# version 06
# baseline (speedup 1.0000x reference)
import numpy as np
import ml_dtypes
import concourse.bass as bass
import concourse.mybir as mybir
from concourse.ap import AP
from concourse.bass_utils import run_bass_kernel_spmd
from contextlib import ExitStack

F32 = mybir.dt.float32
BF16 = mybir.dt.bfloat16
AF = mybir.ActivationFunctionType
ALU = mybir.AluOpType
AXL = mybir.AxisListType

NCORE = 8
NTOK = 2048
NTT = NTOK // 128
DM = 4096
ND = DM // 128
HID = 11008
NH = HID // 128
R = 16
EPS = 1e-5


class Sem:
    __slots__ = ("name", "n", "h")

    def __init__(self, name):
        self.name, self.n, self.h = name, 0, None


class Plan:
    def __init__(self):
        self.ops = []

    def op(self, eng, fn, waits=(), inc=None):
        waits = [(s, v) for (s, v) in waits if v > 0]
        incs = []
        if inc is not None:
            s, k = inc if isinstance(inc, tuple) else (inc, 1)
            s.n += k
            incs = [(s, k)]
        self.ops.append((eng, fn, waits, incs))

    def check(self, sems):
        tot = {s.name: 0 for s in sems}
        mx = {s.name: 0 for s in sems}
        for (_, _, waits, incs) in self.ops:
            for (s, v) in waits:
                mx[s.name] = max(mx[s.name], v)
            for (s, k) in incs:
                tot[s.name] += k
        for s in sems:
            assert mx[s.name] <= tot[s.name], \
                f"sem {s.name}: max wait {mx[s.name]} > total incs {tot[s.name]}"
        engs = {}
        for o in self.ops:
            engs.setdefault(o[0], []).append(o)
        ptr = {e: 0 for e in engs}
        cur = {s.name: 0 for s in sems}
        progress = True
        while progress:
            progress = False
            for e, lst in engs.items():
                while ptr[e] < len(lst):
                    (_, _, waits, incs) = lst[ptr[e]]
                    if all(cur[s.name] >= v for (s, v) in waits):
                        for (s, k) in incs:
                            cur[s.name] += k
                        ptr[e] += 1
                        progress = True
                    else:
                        break
        stuck = {e: ptr[e] for e in engs if ptr[e] < len(engs[e])}
        if stuck:
            msgs = []
            for e in stuck:
                (_, _, waits, _) = engs[e][ptr[e]]
                blocked = [(s.name, v, cur[s.name]) for (s, v) in waits if cur[s.name] < v]
                msgs.append(f"{e}@{ptr[e]}/{len(engs[e])} blocked on {blocked}")
            raise AssertionError("plan deadlock: " + "; ".join(msgs))


def _host_consts():
    li = np.arange(4096)
    tau = np.tan(np.pi * li / 4096.0)
    cq = np.arange(32)
    pp = np.arange(128)
    W32 = np.exp(2j * np.pi * np.outer(cq, cq) / 32.0)
    TW = np.exp(2j * np.pi * np.outer(pp, cq) / 4096.0)
    W128 = np.exp(2j * np.pi * np.outer(pp, pp) / 128.0) / 4096.0
    f32 = lambda x: np.ascontiguousarray(x, np.float32)
    return dict(
        tau64=f32(np.broadcast_to(tau, (64, 4096))),
        taupl=f32(tau.reshape(32, 128).T),
        w32cat=f32(np.concatenate([W32.real, W32.imag, -W32.imag], axis=1)),
        twcat=f32(np.concatenate([TW.real, TW.imag], axis=1)),
        w128a=f32(W128.real), w128b=f32(-W128.imag),
        ident=np.eye(128, dtype=ml_dtypes.bfloat16),
        identf=np.eye(128, dtype=np.float32),
        jmat=np.eye(128, dtype=ml_dtypes.bfloat16)[::-1].copy(),
        ones64=np.ones((1, 64), np.float32),
    )


def build_nc():
    nc = bass.Bass("TRN2", target_bir_lowering=False)
    C = _host_consts()

    u_d = nc.dram_tensor("u", [NTOK, DM], F32, kind="ExternalInput")
    w1T_d = nc.dram_tensor("w1T", [DM, HID], F32, kind="ExternalInput")
    w3T_d = nc.dram_tensor("w3T", [DM, HID], F32, kind="ExternalInput")
    w2T_d = nc.dram_tensor("w2T", [HID, DM], F32, kind="ExternalInput")
    a1T_d = nc.dram_tensor("a1T", [128, ND * R], F32, kind="ExternalInput")
    a3T_d = nc.dram_tensor("a3T", [128, ND * R], F32, kind="ExternalInput")
    a2T_d = nc.dram_tensor("a2T", [128, NH * R], F32, kind="ExternalInput")
    b1T_d = nc.dram_tensor("b1T", [R, HID], F32, kind="ExternalInput")
    b3T_d = nc.dram_tensor("b3T", [R, HID], F32, kind="ExternalInput")
    b2T_d = nc.dram_tensor("b2T", [R, DM], F32, kind="ExternalInput")
    lam_d = nc.dram_tensor("lam", [64, 2], F32, kind="ExternalInput")
    pri_d = nc.dram_tensor("pri", [64, 2], F32, kind="ExternalInput")
    qri_d = nc.dram_tensor("qri", [64, 2], F32, kind="ExternalInput")
    vc_d = nc.dram_tensor("vc", [64, 128], F32, kind="ExternalInput")
    ct_d = nc.dram_tensor("ct", [64, 2], F32, kind="ExternalInput")
    bp_d = nc.dram_tensor("bp", [1, 64], F32, kind="ExternalInput")
    dd_d = nc.dram_tensor("dd", [1, 1], F32, kind="ExternalInput")
    ls_d = nc.dram_tensor("ls", [1, 1], F32, kind="ExternalInput")
    out_d = nc.dram_tensor("out", [NTOK, DM], F32, kind="ExternalOutput")

    gscr_d = nc.dram_tensor("gscr", [NTT * NH * 128 * 128], BF16, kind="Internal")
    sscr_d = nc.dram_tensor("sscr", [NTT * 128 * DM], F32, kind="Internal")
    kpad_d = nc.dram_tensor("kpad", [4352], F32, kind="Internal")

    cd = {k: nc.inline_tensor(np.ascontiguousarray(v), name="c_" + k) for k, v in C.items()}

    sems = []

    def S(name):
        s = Sem(name)
        sems.append(s)
        return s

    kg = S("kg"); kgh = S("kgh"); kgs = S("kgs"); alod = S("alod")
    uld = [S("uld0"), S("uld1")]
    lnv = S("lnv"); lns = S("lns"); appl = S("appl")
    ttr = S("ttr"); tev = S("tev"); axm = S("axm"); axe = S("axe")
    smm = S("smm"); sev = S("sev")
    sst = [S("sst0"), S("sst1")]
    wld = [S("wld0"), S("wld1")]
    wfr = S("wfr"); mmh1 = S("mmh1"); mmh3 = S("mmh3")
    silS = S("sil"); gvec = S("gvec")
    gst = [S("gst0"), S("gst1")]
    w2ld = [S("w2ld0"), S("w2ld1")]
    w2fr = S("w2fr")
    gpl = [S("gpl0"), S("gpl1"), S("gpl2")]
    suld = [S("suld0"), S("suld1")]
    ymm = S("ymm"); a2rq = S("a2rq"); a2ev = S("a2ev"); yev = S("yev")
    tbm = S("tbm"); tbe = S("tbe")
    kgv = S("kgv"); kgt = S("kgt"); kge = S("kge")
    kgu = [S("kgu0"), S("kgu1")]
    sil2 = S("sil2"); p2v = S("p2v"); epi = S("epi")
    outst = [S("outst0"), S("outst1")]

    P = Plan()

    with ExitStack() as stack:
        E = stack.enter_context
        s_ident = E(nc.sbuf_tensor("s_ident", [128, 128], BF16))
        s_eps = E(nc.sbuf_tensor("s_eps", [128, 1], F32))

        stA = E(ExitStack())
        EA = stA.enter_context
        s_xnT = EA(nc.sbuf_tensor("s_xnT", [128, ND * NTOK], BF16))
        s_a1T = EA(nc.sbuf_tensor("s_a1T", [128, ND * R], BF16))
        s_a3T = EA(nc.sbuf_tensor("s_a3T", [128, ND * R], BF16))
        s_a1s = EA(nc.sbuf_tensor("s_a1s", [R, NTOK], BF16))
        s_a3s = EA(nc.sbuf_tensor("s_a3s", [R, NTOK], BF16))

        def xt(d, lo, n):
            return s_xnT[:, d * NTOK + lo: d * NTOK + lo + n]

        s_TB = EA(nc.sbuf_tensor("s_TB", [128, 32 * 128], BF16))

        stKG = EA(ExitStack())
        EK = stKG.enter_context
        s_identf = EK(nc.sbuf_tensor("s_identf", [128, 128], F32))
        s_J = EK(nc.sbuf_tensor("s_J", [128, 128], BF16))
        s_one1 = EK(nc.sbuf_tensor("s_one1", [1, 64], F32))
        s_par = EK(nc.sbuf_tensor("s_par", [64, 24], F32))
        s_sc = EK(nc.sbuf_tensor("s_sc", [1, 8], F32))
        s_bb = EK(nc.sbuf_tensor("s_bb", [64, 64], F32))
        s_bt = EK(nc.sbuf_tensor("s_bt", [64, 64], F32))
        s_vc = EK(nc.sbuf_tensor("s_vc", [64, 128], F32))
        s_lh = EK(nc.sbuf_tensor("s_lh", [64, 16], F32))
        s_tau = [EK(nc.sbuf_tensor(f"s_tau{b}", [64, 512], F32)) for b in range(2)]
        s_dimb = EK(nc.sbuf_tensor("s_dimb", [64, 512], F32))
        s_t1 = EK(nc.sbuf_tensor("s_t1", [64, 512], F32))
        s_t2 = EK(nc.sbuf_tensor("s_t2", [64, 512], F32))
        s_ire = [EK(nc.sbuf_tensor(f"s_ire{b}", [64, 512], F32)) for b in range(2)]
        s_iim = [EK(nc.sbuf_tensor(f"s_iim{b}", [64, 512], F32)) for b in range(2)]
        s_kst = [EK(nc.sbuf_tensor(f"s_kst{b}", [8, 512], F32)) for b in range(2)]
        s_kT = EK(nc.sbuf_tensor("s_kT", [128, 32 * 8], F32))
        s_ar = EK(nc.sbuf_tensor("s_ar", [128, 14 * 32], F32))
        s_tpl = EK(nc.sbuf_tensor("s_tpl", [128, 32], F32))
        s_idf = EK(nc.sbuf_tensor("s_idf", [32, 256], F32))
        s_w32 = EK(nc.sbuf_tensor("s_w32", [32, 96], F32))
        s_tw = EK(nc.sbuf_tensor("s_tw", [128, 64], F32))
        s_w128a = EK(nc.sbuf_tensor("s_w128a", [128, 128], F32))
        s_w128b = EK(nc.sbuf_tensor("s_w128b", [128, 128], F32))
        s_A = EK(nc.sbuf_tensor("s_A", [128, 64], F32))
        s_B2 = EK(nc.sbuf_tensor("s_B2", [128, 64], F32))
        s_ksb = EK(nc.sbuf_tensor("s_ksb", [128, 32], F32))
        s_z = EK(nc.sbuf_tensor("s_z", [1, 128], F32))
        s_hc = EK(nc.sbuf_tensor("s_hc", [128, 32 * 128], BF16))

        pKG = EK(ExitStack())
        p_kg = pKG.enter_context(nc.psum_tensor("p_kg", [64, 64], F32))
        p_k8 = pKG.enter_context(nc.psum_tensor("p_k8", [8, 512], F32))
        p_ktr = pKG.enter_context(nc.psum_tensor("p_ktr", [128, 8], F32))
        p_t32 = pKG.enter_context(nc.psum_tensor("p_t32", [32, 128], F32))
        p_A = pKG.enter_context(nc.psum_tensor("p_A", [128, 64], F32))
        p_kq = pKG.enter_context(nc.psum_tensor("p_kq", [128, 32], F32))
        p_tb = [pKG.enter_context(nc.psum_tensor(f"p_tb{b}", [128, 128], F32)) for b in range(2)]

        chain_state = [None]

        def ch(eng, fn, dma=False):
            w = [chain_state[0]] if chain_state[0] else []
            if dma:
                s = kgs if eng == "gpsimd" else kgh
                P.op(eng, fn, waits=w, inc=(s, 16))
                chain_state[0] = (s, s.n)
            else:
                P.op(eng, fn, waits=w, inc=(kg, 1))
                chain_state[0] = (kg, kg.n)

        pcv = lambda j: s_par[:, j:j + 1]

        def hdr(fn):
            P.op("sync", fn, inc=(kgh, 16))

        hdr(lambda e: e.dma_start(out=s_ident[:, :], in_=cd["ident"].ap()))
        hdr(lambda e: e.dma_start(out=s_identf[:, :], in_=cd["identf"].ap()))
        hdr(lambda e: e.dma_start(out=s_J[:, :], in_=cd["jmat"].ap()))
        hdr(lambda e: e.dma_start(out=s_one1[:, :], in_=cd["ones64"].ap()))
        hdr(lambda e: e.dma_start(out=s_tpl[:, :], in_=cd["taupl"].ap()))
        hdr(lambda e: e.dma_start(out=s_w32[:, :], in_=cd["w32cat"].ap()))
        hdr(lambda e: e.dma_start(out=s_tw[:, :], in_=cd["twcat"].ap()))
        hdr(lambda e: e.dma_start(out=s_w128a[:, :], in_=cd["w128a"].ap()))
        hdr(lambda e: e.dma_start(out=s_w128b[:, :], in_=cd["w128b"].ap()))
        hdr(lambda e: e.dma_start(out=s_par[:, 0:2], in_=lam_d.ap()))
        hdr(lambda e: e.dma_start(out=s_par[:, 2:4], in_=pri_d.ap()))
        hdr(lambda e: e.dma_start(out=s_par[:, 4:6], in_=qri_d.ap()))
        hdr(lambda e: e.dma_start(out=s_par[:, 6:8], in_=ct_d.ap()))
        hdr(lambda e: e.dma_start(out=s_vc[:, :], in_=vc_d.ap()))
        hdr(lambda e: e.dma_start(out=s_bb[0:1, :], in_=bp_d.ap()))
        hdr(lambda e: e.dma_start(out=s_sc[:, 3:4], in_=dd_d.ap()))
        hdr(lambda e: e.dma_start(out=s_sc[:, 0:1], in_=ls_d.ap()))
        HDRT = kgh.n
        P.op("gpsimd", lambda e: e.dma_start(out=s_a1T[:, :], in_=a1T_d.ap()),
             inc=(alod, 16))
        P.op("gpsimd", lambda e: e.dma_start(out=s_a3T[:, :], in_=a3T_d.ap()),
             inc=(alod, 16))

        P.op("vector", lambda e: e.memset(s_eps[:, :], EPS), inc=(kg, 1))
        P.op("scalar", lambda e: e.activation(s_sc[:, 1:2], s_sc[:, 0:1], AF.Exp, scale=-1.0),
             waits=[(kg, kg.n), (kgh, HDRT)], inc=(kg, 1))
        chain_state[0] = (kg, kg.n)
        ch("scalar", lambda e: e.activation(s_sc[:, 2:3], s_sc[:, 1:2], AF.Copy, scale=2.0))
        ch("tensor", lambda e: e.matmul(p_kg[:, 0:1], s_one1[:, :], s_sc[:, 2:3],
                                        start=True, stop=True))
        ch("vector", lambda e: e.tensor_copy(pcv(18), p_kg[:, 0:1]))
        ch("tensor", lambda e: e.matmul(p_kg[:, :], s_one1[:, :], s_bb[0:1, :],
                                        start=True, stop=True))
        ch("vector", lambda e: e.tensor_copy(s_bb[:, :], p_kg[:, :]))
        vre = s_vc[:, :].rearrange("p (n two) -> p n two", two=2)[:, :, 0]
        vim = s_vc[:, :].rearrange("p (n two) -> p n two", two=2)[:, :, 1]
        ch("vector", lambda e: e.tensor_tensor(s_bt[:, :], vre, s_bb[:, :], op=ALU.mult))
        ch("vector", lambda e: e.reduce_sum(pcv(8), s_bt[:, :], axis=AXL.X))
        ch("vector", lambda e: e.tensor_tensor(s_bt[:, :], vim, s_bb[:, :], op=ALU.mult))
        ch("vector", lambda e: e.reduce_sum(pcv(9), s_bt[:, :], axis=AXL.X))
        for (dst, c1, c2, x1, x2) in [(10, 6, 7, 8, 9), (11, 6, 7, 2, 3),
                                      (12, 4, 5, 8, 9), (13, 4, 5, 2, 3)]:
            ch("vector", lambda e, c1=c1, x1=x1: e.tensor_tensor(pcv(20), pcv(c1), pcv(x1), op=ALU.mult))
            ch("vector", lambda e, c2=c2, x2=x2: e.tensor_tensor(pcv(21), pcv(c2), pcv(x2), op=ALU.mult))
            ch("vector", lambda e, dst=dst: e.tensor_tensor(pcv(dst), pcv(20), pcv(21), op=ALU.add))
        for (dst, c1, c2, x1, x2) in [(14, 6, 7, 8, 9), (15, 6, 7, 2, 3),
                                      (16, 4, 5, 8, 9), (17, 4, 5, 2, 3)]:
            ch("vector", lambda e, c1=c1, x2=x2: e.tensor_tensor(pcv(20), pcv(c1), pcv(x2), op=ALU.mult))
            ch("vector", lambda e, c2=c2, x1=x1: e.tensor_tensor(pcv(21), pcv(c2), pcv(x1), op=ALU.mult))
            ch("vector", lambda e, dst=dst: e.tensor_tensor(pcv(dst), pcv(20), pcv(21), op=ALU.subtract))
        ch("vector", lambda e: e.tensor_tensor(pcv(19), pcv(0), pcv(0), op=ALU.mult))
        ch("vector", lambda e: e.tensor_scalar_mul(pcv(22), pcv(0), -1.0))
        for j in range(8):
            ch("vector", lambda e, j=j: e.tensor_copy(s_lh[:, j:j + 1], pcv(10 + j)))
        for j in range(4):
            ch("vector", lambda e, j=j: e.tensor_scalar_mul(s_lh[:, 8 + j:9 + j], pcv(14 + j), -1.0))
        for j in range(4):
            ch("vector", lambda e, j=j: e.tensor_copy(s_lh[:, 12 + j:13 + j], pcv(10 + j)))

        KGPRE = kg.n
        for cc in range(8):
            bc = cc % 2
            w = [(kgv, 6 * (cc - 1))] if cc >= 2 else []
            P.op("sync", lambda e, cc=cc, bc=bc: e.dma_start(
                out=s_tau[bc][:, :],
                in_=AP(cd["tau64"], cc * 512, [[4096, 64], [1, 512]])),
                waits=w, inc=(kgu[bc], 16))
        for cc in range(8):
            bc = cc % 2
            w = [(kgu[bc], 16 * (cc // 2 + 1)), (kgv, 6 * cc)]
            if cc == 0:
                w.append((kg, KGPRE))
            P.op("vector", lambda e, bc=bc: e.tensor_scalar(
                s_dimb[:, :], s_tau[bc][:, :], pcv(18), pcv(1),
                op0=ALU.mult, op1=ALU.subtract), waits=w, inc=kgv)
            P.op("vector", lambda e: e.tensor_tensor(
                s_t1[:, :], s_dimb[:, :], s_dimb[:, :], op=ALU.mult),
                waits=[(kgv, 6 * cc + 1)], inc=kgv)
            P.op("vector", lambda e: e.tensor_scalar(
                s_t2[:, :], s_t1[:, :], pcv(19), None, op0=ALU.add),
                waits=[(kgv, 6 * cc + 2)], inc=kgv)
            P.op("vector", lambda e: e.reciprocal(s_t1[:, :], s_t2[:, :]),
                 waits=[(kgv, 6 * cc + 3)], inc=kgv)
            w = [(kgv, 6 * cc + 4)] + ([(kgt, 6 * (cc - 2) + 2)] if cc >= 2 else [])
            P.op("vector", lambda e, bc=bc: e.tensor_scalar(
                s_ire[bc][:, :], s_t1[:, :], pcv(22), None, op0=ALU.mult),
                waits=w, inc=kgv)
            P.op("vector", lambda e, bc=bc: e.scalar_tensor_tensor(
                s_iim[bc][:, :], s_dimb[:, :], -1.0, s_t1[:, :],
                op0=ALU.mult, op1=ALU.mult),
                waits=[(kgv, 6 * cc + 5)], inc=kgv)
            w = [(kgv, 6 * cc + 6)] + ([(kge, 5 * (cc - 1) + 1)] if cc >= 1 else [])
            P.op("tensor", lambda e, bc=bc: e.matmul(p_k8[:, :], s_lh[:, 0:8],
                                                     s_ire[bc][:, :],
                                                     start=True, stop=False),
                 waits=w, inc=kgt)
            P.op("tensor", lambda e, bc=bc: e.matmul(p_k8[:, :], s_lh[:, 8:16],
                                                     s_iim[bc][:, :],
                                                     start=False, stop=True),
                 inc=kgt)
            P.op("scalar", lambda e, bc=bc: e.copy(s_kst[bc][:, :], p_k8[:, :]),
                 waits=[(kgt, 6 * cc + 2)], inc=kge)
            for j in range(4):
                P.op("tensor", lambda e, bc=bc, j=j: e.transpose(
                    p_ktr[:, :], s_kst[bc][:, j * 128:(j + 1) * 128],
                    s_identf[0:8, 0:8]),
                    waits=[(kge, 5 * cc + 1 + j)], inc=kgt)
                P.op("scalar", lambda e, cc=cc, j=j: e.copy(
                    s_kT[:, (4 * cc + j) * 8:(4 * cc + j) * 8 + 8], p_ktr[:, :]),
                    waits=[(kgt, 6 * cc + 3 + j)], inc=kge)
        chain_state[0] = (kge, 40)

        kv = lambda j: s_kT[:, :].rearrange("p (c k) -> p c k", k=8)[:, :, j]
        av = lambda j: s_ar[:, j * 32:(j + 1) * 32]
        ch("vector", lambda e: e.tensor_scalar(av(0), kv(3), 1.0, None, op0=ALU.add))
        ch("vector", lambda e: e.tensor_tensor(av(1), av(0), av(0), op=ALU.mult))
        ch("vector", lambda e: e.tensor_tensor(av(13), kv(7), kv(7), op=ALU.mult))
        ch("vector", lambda e: e.tensor_tensor(av(1), av(1), av(13), op=ALU.add))
        ch("vector", lambda e: e.reciprocal(av(2), av(1)))
        ch("vector", lambda e: e.tensor_tensor(av(3), av(0), av(2), op=ALU.mult))
        ch("vector", lambda e: e.scalar_tensor_tensor(av(4), kv(7), -1.0, av(2),
                                                      op0=ALU.mult, op1=ALU.mult))
        ch("vector", lambda e: e.tensor_tensor(av(5), kv(1), kv(2), op=ALU.mult))
        ch("vector", lambda e: e.tensor_tensor(av(13), kv(5), kv(6), op=ALU.mult))
        ch("vector", lambda e: e.tensor_tensor(av(5), av(5), av(13), op=ALU.subtract))
        ch("vector", lambda e: e.tensor_tensor(av(6), kv(1), kv(6), op=ALU.mult))
        ch("vector", lambda e: e.tensor_tensor(av(13), kv(5), kv(2), op=ALU.mult))
        ch("vector", lambda e: e.tensor_tensor(av(6), av(6), av(13), op=ALU.add))
        ch("vector", lambda e: e.tensor_tensor(av(7), av(5), av(3), op=ALU.mult))
        ch("vector", lambda e: e.tensor_tensor(av(13), av(6), av(4), op=ALU.mult))
        ch("vector", lambda e: e.tensor_tensor(av(7), av(7), av(13), op=ALU.subtract))
        ch("vector", lambda e: e.tensor_tensor(av(8), av(5), av(4), op=ALU.mult))
        ch("vector", lambda e: e.tensor_tensor(av(13), av(6), av(3), op=ALU.mult))
        ch("vector", lambda e: e.tensor_tensor(av(8), av(8), av(13), op=ALU.add))
        ch("vector", lambda e: e.tensor_tensor(av(9), kv(0), av(7), op=ALU.subtract))
        ch("vector", lambda e: e.tensor_tensor(av(10), kv(4), av(8), op=ALU.subtract))
        ch("vector", lambda e: e.tensor_tensor(av(13), s_tpl[:, :], av(10), op=ALU.mult))
        ch("vector", lambda e: e.tensor_tensor(av(11), av(9), av(13), op=ALU.subtract))
        ch("vector", lambda e: e.tensor_tensor(av(13), s_tpl[:, :], av(9), op=ALU.mult))
        ch("vector", lambda e: e.tensor_tensor(av(12), av(10), av(13), op=ALU.add))

        ch("tensor", lambda e: e.transpose(p_t32[:, :], av(11), s_identf[:, :]))
        ch("vector", lambda e: e.tensor_copy(s_idf[:, 0:128], p_t32[:, :]))
        ch("tensor", lambda e: e.transpose(p_t32[:, :], av(12), s_identf[:, :]))
        ch("vector", lambda e: e.tensor_copy(s_idf[:, 128:256], p_t32[:, :]))
        ch("tensor", lambda e: e.matmul(p_A[:, 0:32], s_idf[:, 0:128], s_w32[:, 0:32],
                                        start=True, stop=False))
        ch("tensor", lambda e: e.matmul(p_A[:, 0:32], s_idf[:, 128:256], s_w32[:, 64:96],
                                        start=False, stop=True))
        ch("tensor", lambda e: e.matmul(p_A[:, 32:64], s_idf[:, 0:128], s_w32[:, 32:64],
                                        start=True, stop=False))
        ch("tensor", lambda e: e.matmul(p_A[:, 32:64], s_idf[:, 128:256], s_w32[:, 0:32],
                                        start=False, stop=True))
        ch("vector", lambda e: e.tensor_copy(s_A[:, :], p_A[:, :]))
        ch("vector", lambda e: e.tensor_tensor(s_B2[:, 0:32], s_A[:, 0:32], s_tw[:, 0:32], op=ALU.mult))
        ch("vector", lambda e: e.tensor_tensor(av(13), s_A[:, 32:64], s_tw[:, 32:64], op=ALU.mult))
        ch("vector", lambda e: e.tensor_tensor(s_B2[:, 0:32], s_B2[:, 0:32], av(13), op=ALU.subtract))
        ch("vector", lambda e: e.tensor_tensor(s_B2[:, 32:64], s_A[:, 0:32], s_tw[:, 32:64], op=ALU.mult))
        ch("vector", lambda e: e.tensor_tensor(av(13), s_A[:, 32:64], s_tw[:, 0:32], op=ALU.mult))
        ch("vector", lambda e: e.tensor_tensor(s_B2[:, 32:64], s_B2[:, 32:64], av(13), op=ALU.add))
        ch("tensor", lambda e: e.matmul(p_kq[:, :], s_w128a[:, :], s_B2[:, 0:32],
                                        start=True, stop=False))
        ch("tensor", lambda e: e.matmul(p_kq[:, :], s_w128b[:, :], s_B2[:, 32:64],
                                        start=False, stop=True))
        ch("vector", lambda e: e.tensor_copy(s_ksb[:, :], p_kq[:, :]))
        ch("vector", lambda e: e.tensor_tensor(s_ksb[0:1, 0:1], s_ksb[0:1, 0:1],
                                               s_sc[:, 3:4], op=ALU.add))
        ch("vector", lambda e: e.memset(s_z[:, :], 0.0))
        ch("sync", lambda e: e.dma_start(out=AP(kpad_d, 0, [[128, 1], [1, 128]]),
                                         in_=s_z[:, :]), dma=True)
        ch("sync", lambda e: e.dma_start(out=AP(kpad_d, 4224, [[128, 1], [1, 128]]),
                                         in_=s_z[:, :]), dma=True)
        ch("sync", lambda e: e.dma_start(out=AP(kpad_d, 128, [[32, 128], [1, 32]]),
                                         in_=s_ksb[:, :]), dma=True)
        ch("gpsimd", lambda e: e.dma_start(
            out=s_hc[:, :].rearrange("p (d j) -> p d j", d=32),
            in_=AP(kpad_d, 1, [[1, 128], [128, 32], [1, 128]])), dma=True)
        hc_state = chain_state[0]
        for d in range(32):
            w = [hc_state] if d == 0 else ([(tbe, d - 1)] if d >= 2 else [])
            P.op("tensor", lambda e, d=d: e.matmul(p_tb[d % 2][:, :], s_J[:, :],
                                                   s_hc[:, d * 128:(d + 1) * 128],
                                                   start=True, stop=True),
                 waits=w, inc=tbm)
            P.op("vector", lambda e, d=d: e.tensor_copy(
                s_TB[:, d * 128:(d + 1) * 128], p_tb[d % 2][:, :]),
                waits=[(tbm, d + 1)], inc=tbe)
        pKG.close()
        stKG.close()
        KGF = kg.n

        stLN = EA(ExitStack())
        EL = stLN.enter_context
        s_u = [EL(nc.sbuf_tensor(f"s_u{b}", [128, DM], F32)) for b in range(2)]
        s_xn = [EL(nc.sbuf_tensor(f"s_xn{b}", [128, DM], BF16)) for b in range(3)]
        s_st = [EL(nc.sbuf_tensor(f"s_st{b}", [128, 8], F32)) for b in range(2)]
        s_sst = [EL(nc.sbuf_tensor(f"s_sstg{b}", [128, 512], F32)) for b in range(2)]
        p_ln = [EL(nc.psum_tensor(f"p_ln{b}", [128, 128], BF16)) for b in range(4)]
        p_s = [EL(nc.psum_tensor(f"p_s{b}", [128, 512], F32)) for b in range(2)]

        def emit_ln_stats(i):
            b = i % 2
            b4 = i % 3
            w = [(kg, KGF), (tbm, 32)] if i < 2 else [(appl, i - 1), (lns, 3 * (i - 2) + 1)]
            P.op("sync", lambda e, i=i, b=b: e.dma_start(
                out=s_u[b][:, :], in_=AP(u_d, i * 128 * DM, [[DM, 128], [1, DM]])),
                waits=w, inc=(uld[b], 16))
            uldw = [(uld[b], 16 * (i // 2 + 1))]
            P.op("vector", lambda e, b=b: e.reduce_sum(s_st[b][:, 0:1], s_u[b][:, :], axis=AXL.X),
                 waits=uldw, inc=lnv)
            P.op("scalar", lambda e, b=b, b4=b4: e.activation(
                s_xn[b4][:, :], s_u[b][:, :], AF.Square, accum_out=s_st[b][:, 1:2]),
                waits=uldw + ([(ttr, 32 * (i - 2))] if i >= 3 else []),
                inc=lns)
            P.op("vector", lambda e, b=b: e.tensor_scalar_mul(s_st[b][:, 2:3], s_st[b][:, 0:1],
                                                              1.0 / DM),
                 waits=[(lnv, 4 * i + 1)], inc=lnv)
            P.op("vector", lambda e, b=b: e.tensor_tensor(s_st[b][:, 3:4], s_st[b][:, 2:3],
                                                          s_st[b][:, 2:3], op=ALU.mult),
                 waits=[(lnv, 4 * i + 2)], inc=lnv)
            P.op("vector", lambda e, b=b: e.tensor_scalar(
                s_st[b][:, 4:5], s_st[b][:, 1:2], 1.0 / DM, s_st[b][:, 3:4],
                op0=ALU.mult, op1=ALU.subtract),
                waits=[(lnv, 4 * i + 3), (lns, 3 * i + 1)], inc=lnv)
            P.op("scalar", lambda e, b=b: e.activation(s_st[b][:, 5:6], s_st[b][:, 4:5],
                                                       AF.Ln, bias=s_eps[:, :]),
                 waits=[(lnv, 4 * i + 4), (lns, 3 * i + 1)], inc=lns)
            P.op("scalar", lambda e, b=b: e.activation(s_st[b][:, 6:7], s_st[b][:, 5:6],
                                                       AF.Exp, scale=-0.5),
                 waits=[(lns, 3 * i + 2)], inc=lns)
            P.op("vector", lambda e, b=b, b4=b4: e.tensor_scalar(
                s_xn[b4][:, :], s_u[b][:, :], s_st[b][:, 2:3], s_st[b][:, 6:7],
                op0=ALU.subtract, op1=ALU.mult),
                waits=[(lns, 3 * i + 3)], inc=appl)

        def emit_ln_T(i):
            b4 = i % 3
            for d in range(ND):
                gidx = 32 * i + d
                w = [(appl, i + 1)] if d == 0 else []
                if gidx >= 4:
                    w = w + [(tev, gidx - 3)]
                P.op("tensor", lambda e, b4=b4, d=d, gidx=gidx: e.transpose(
                    p_ln[gidx % 4][:, :], s_xn[b4][:, d * 128:(d + 1) * 128], s_ident[:, :]),
                    waits=w, inc=ttr)
                P.op("vector", lambda e, i=i, d=d, gidx=gidx: e.tensor_copy(
                    xt(d, i * 128, 128), p_ln[gidx % 4][:, :]),
                    waits=[(ttr, gidx + 1)], inc=tev)

        def emit_conv(it):
            for g in range(8):
                idx = it * 8 + g
                bank = idx % 2
                for i in range(4 * g + 4):
                    w = []
                    if i == 0:
                        w = [(sev, idx - 1)] if idx >= 2 else []
                        w = w + [(tev, 32 * (it + 1))]
                        if idx == 0:
                            w = w + [(tbe, 32)]
                    k = i - 4 * g
                    if k <= 0:
                        dlo, clo, n = 4 * g - i, 0, 512
                    else:
                        dlo, clo, n = 0, k * 128, (4 - k) * 128
                    P.op("tensor", lambda e, it=it, i=i, bank=bank,
                         dlo=dlo, clo=clo, n=n, g=g: e.matmul(
                        p_s[bank][:, clo:clo + n], xt(i, it * 128, 128),
                        s_TB[:, dlo * 128:dlo * 128 + n],
                        start=(i == 0), stop=(i == 4 * g + 3),
                        skip_group_check=True),
                        waits=w, inc=(smm, 1) if i == 4 * g + 3 else None)
                P.op("scalar", lambda e, bank=bank: e.copy(s_sst[bank][:, :], p_s[bank][:, :]),
                     waits=[(smm, idx + 1)] +
                           ([(sst[bank], 16 * (idx // 2))] if idx >= 2 else []),
                     inc=sev)
                P.op("sync", lambda e, it=it, g=g, bank=bank: e.dma_start(
                    out=AP(sscr_d, it * 128 * DM + g * 512, [[DM, 128], [1, 512]]),
                    in_=s_sst[bank][:, :]),
                    waits=[(sev, idx + 1)], inc=(sst[bank], 16))

        for step in range(NTT + 4):
            if step < NTT:
                emit_ln_stats(step)
            if 2 <= step < NTT + 2:
                emit_ln_T(step - 2)
            if step >= 4:
                emit_conv(step - 4)
        assert tev.n == 512 and appl.n == 16 and smm.n == 128

        stLN.close()
        stAX = EA(ExitStack())
        p_ax = stAX.enter_context(nc.psum_tensor("p_ax", [R, NTOK], F32))
        for wsel, (aT, asb) in enumerate(((s_a1T, s_a1s), (s_a3T, s_a3s))):
            for tb in range(4):
                for d in range(ND):
                    w = []
                    if tb == 0 and d == 0:
                        w = [(tev, 512), (alod, 32), (sev, 128)] + \
                            ([(axe, 1)] if wsel == 1 else [])
                    last = (tb == 3 and d == ND - 1)
                    P.op("tensor", lambda e, aT=aT, tb=tb, d=d: e.matmul(
                        p_ax[:, tb * 512:(tb + 1) * 512], aT[:, d * R:(d + 1) * R],
                        xt(d, tb * 512, 512), start=(d == 0), stop=(d == ND - 1)),
                        waits=w, inc=(axm if last else None))
            P.op("scalar", lambda e, asb=asb: e.activation(asb[:, :], p_ax[:, :],
                                                           AF.Copy, scale=2.0),
                 waits=[(axm, wsel + 1)], inc=axe)

        stAX.close()
        stP1 = EA(ExitStack())
        E1 = stP1.enter_context
        s_w1s = [E1(nc.sbuf_tensor(f"s_w1s{b}", [128, ND * 128], BF16)) for b in range(2)]
        s_w3s = [E1(nc.sbuf_tensor(f"s_w3s{b}", [128, ND * 128], BF16)) for b in range(2)]
        s_b1s = [E1(nc.sbuf_tensor(f"s_b1s{b}", [R, 128], BF16)) for b in range(2)]
        s_b3s = [E1(nc.sbuf_tensor(f"s_b3s{b}", [R, 128], BF16)) for b in range(2)]
        s_sil = E1(nc.sbuf_tensor("s_sil", [128, NTOK], BF16))
        s_g = [E1(nc.sbuf_tensor(f"s_g{b}", [128, NTOK], BF16)) for b in range(2)]
        p_h1 = E1(nc.psum_tensor("p_h1", [128, NTOK], F32))
        p_h3 = E1(nc.psum_tensor("p_h3", [128, NTOK], F32))

        for c in range(NH):
            bw = c % 2
            w = [(wfr, c - 1)] if c >= 2 else \
                [(smm, 128), (sst[0], 16 * 64), (sst[1], 16 * 64)]
            P.op("gpsimd", lambda e, c=c, bw=bw: e.dma_start(
                out=s_w1s[bw][:, :].rearrange("p (d h) -> p d h", d=ND),
                in_=AP(w1T_d, c * 128, [[HID, 128], [128 * HID, ND], [1, 128]])),
                waits=w, inc=(wld[bw], 16))
            P.op("gpsimd", lambda e, c=c, bw=bw: e.dma_start(
                out=s_w3s[bw][:, :].rearrange("p (d h) -> p d h", d=ND),
                in_=AP(w3T_d, c * 128, [[HID, 128], [128 * HID, ND], [1, 128]])),
                inc=(wld[bw], 16))
            P.op("gpsimd", lambda e, c=c, bw=bw: e.dma_start(
                out=s_b1s[bw][:, :], in_=AP(b1T_d, c * 128, [[HID, R], [1, 128]])),
                waits=([(mmh3, 4 * c - 4)] if c >= 2 else []),
                inc=(wld[bw], 16))
            P.op("gpsimd", lambda e, c=c, bw=bw: e.dma_start(
                out=s_b3s[bw][:, :], in_=AP(b3T_d, c * 128, [[HID, R], [1, 128]])),
                inc=(wld[bw], 16))

            for tb in range(4):
                w = [(wld[bw], 64 * (c // 2 + 1))] if tb == 0 else []
                if c >= 1:
                    w = w + [(silS, 4 * (c - 1) + tb + 1)]
                elif tb == 0:
                    w = w + [(axe, 2), (sev, 128)]
                for d in range(ND):
                    P.op("tensor", lambda e, bw=bw, tb=tb, d=d: e.matmul(
                        p_h1[:, tb * 512:(tb + 1) * 512],
                        s_w1s[bw][:, d * 128:(d + 1) * 128], xt(d, tb * 512, 512),
                        start=(d == 0), stop=False),
                        waits=(w if d == 0 else []))
                P.op("tensor", lambda e, bw=bw, tb=tb: e.matmul(
                    p_h1[:, tb * 512:(tb + 1) * 512], s_b1s[bw][:, :],
                    s_a1s[:, tb * 512:(tb + 1) * 512], start=False, stop=True),
                    inc=mmh1)
            for tb in range(4):
                w = [(gvec, 4 * (c - 1) + tb + 1)] if c >= 1 else []
                for d in range(ND):
                    last = (tb == 3 and d == ND - 1)
                    P.op("tensor", lambda e, bw=bw, tb=tb, d=d: e.matmul(
                        p_h3[:, tb * 512:(tb + 1) * 512],
                        s_w3s[bw][:, d * 128:(d + 1) * 128], xt(d, tb * 512, 512),
                        start=(d == 0), stop=False),
                        waits=(w if d == 0 else []), inc=(wfr if last else None))
                P.op("tensor", lambda e, bw=bw, tb=tb: e.matmul(
                    p_h3[:, tb * 512:(tb + 1) * 512], s_b3s[bw][:, :],
                    s_a3s[:, tb * 512:(tb + 1) * 512], start=False, stop=True),
                    inc=mmh3)

            for tb in range(4):
                w = [(mmh1, 4 * c + tb + 1)]
                if c >= 1:
                    w = w + [(gvec, 4 * (c - 1) + tb + 1)]
                P.op("scalar", lambda e, tb=tb: e.activation(
                    s_sil[:, tb * 512:(tb + 1) * 512], p_h1[:, tb * 512:(tb + 1) * 512],
                    AF.Silu), waits=w, inc=silS)
            for tb in range(4):
                w = [(mmh3, 4 * c + tb + 1), (silS, 4 * c + tb + 1)]
                if c >= 2:
                    w = w + [(gst[c % 2], 16 * (c // 2))]
                P.op("vector", lambda e, c=c, tb=tb: e.tensor_tensor(
                    s_g[c % 2][:, tb * 512:(tb + 1) * 512],
                    s_sil[:, tb * 512:(tb + 1) * 512],
                    p_h3[:, tb * 512:(tb + 1) * 512], op=ALU.mult),
                    waits=w, inc=gvec)
            P.op("sync", lambda e, c=c: e.dma_start(
                out=AP(gscr_d, c * 128, [[HID, 128], [128 * HID, NTT], [1, 128]]),
                in_=s_g[c % 2][:, :].rearrange("p (tt t) -> p tt t", tt=NTT)),
                waits=[(gvec, 4 * c + 4)], inc=(gst[c % 2], 16))
        assert gvec.n == 4 * NH and wfr.n == NH

        stP1.close()
        stA.close()
        stB = E(ExitStack())
        E2 = stB.enter_context
        s_w2s = [E2(nc.sbuf_tensor(f"s_w2s{b}", [128, 43 * 512], BF16)) for b in range(2)]
        s_yac = E2(nc.sbuf_tensor("s_yac", [128, NTT * 512], F32))
        s_gp = [E2(nc.sbuf_tensor(f"s_gp{b}", [128, 43 * 128], BF16)) for b in range(3)]
        s_a2T = E2(nc.sbuf_tensor("s_a2T", [128, NH * R], BF16))
        s_a2s = E2(nc.sbuf_tensor("s_a2s", [R, NTOK], BF16))
        s_b2T = E2(nc.sbuf_tensor("s_b2T", [R, DM], BF16))
        s_sb = [E2(nc.sbuf_tensor(f"s_sb{b}", [128, 512], F32)) for b in range(2)]
        s_ub = [E2(nc.sbuf_tensor(f"s_ub{b}", [128, 512], F32)) for b in range(2)]
        s_e1 = [E2(nc.sbuf_tensor(f"s_e1{b}", [128, 512], F32)) for b in range(2)]
        s_s2 = [E2(nc.sbuf_tensor(f"s_s2{b}", [128, 512], F32)) for b in range(2)]
        s_ob = [E2(nc.sbuf_tensor(f"s_ob{b}", [128, 512], F32)) for b in range(2)]
        p_y = [E2(nc.psum_tensor(f"p_y{b}", [128, 512], F32)) for b in range(3)]
        p_a2 = E2(nc.psum_tensor("p_a2", [R, NTOK], F32))

        P1DONE = [(wfr, NH), (gvec, 4 * NH), (silS, 4 * NH),
                  (gst[0], 16 * 43), (gst[1], 16 * 43)]
        P.op("gpsimd", lambda e: e.dma_start(out=s_a2T[:, :], in_=a2T_d.ap()),
             waits=P1DONE, inc=(w2ld[0], 16))
        P.op("gpsimd", lambda e: e.dma_start(out=s_b2T[:, :], in_=b2T_d.ap()),
             inc=(w2ld[0], 16))
        for Q in range(8):
            for hf in range(2):
                sidx = 2 * Q + hf
                par = sidx % 2
                P.op("gpsimd", lambda e, Q=Q, hf=hf, par=par: e.dma_start(
                    out=s_w2s[par][:, :].rearrange("p (ht j) -> p ht j", ht=43),
                    in_=AP(w2T_d, (hf * 43 * 128) * DM + Q * 512,
                           [[DM, 128], [128 * DM, 43], [1, 512]])),
                    waits=([(ymm, 16 * (sidx - 1))] if sidx >= 2 else []),
                    inc=(w2ld[par], 16))

        first_sync_p2 = True
        first_mm_p2 = True
        for Q in range(8):
            for hf in range(2):
                sidx = 2 * Q + hf
                par = sidx % 2
                for t in range(NTT):
                    idx3 = sidx * NTT + t
                    par2 = idx3 % 3
                    idx4 = Q * NTT + t
                    bank = idx3 % 3
                    w = [(ymm, idx3 - 2)] if idx3 >= 3 else []
                    if first_sync_p2:
                        w = list(P1DONE)
                        first_sync_p2 = False
                    P.op("sync", lambda e, t=t, hf=hf, par2=par2: e.dma_start(
                        out=s_gp[par2][:, :],
                        in_=AP(gscr_d, t * 128 * HID + hf * 43 * 128,
                               [[HID, 128], [1, 43 * 128]])),
                        waits=w, inc=(gpl[par2], 16))
                    if hf == 1:
                        bp4 = idx4 % 2
                        w = [(p2v, idx4 - 1)] if idx4 >= 2 else []
                        if idx4 == 0:
                            w = list(P1DONE)
                        P.op("scalar", lambda e, Q=Q, t=t, bp4=bp4: e.dma_start(
                            out=s_sb[bp4][:, :],
                            in_=AP(sscr_d, t * 128 * DM + Q * 512, [[DM, 128], [1, 512]])),
                            waits=w, inc=(suld[bp4], 16))
                        w = [(epi, idx4 - 1)] if idx4 >= 2 else []
                        P.op("scalar", lambda e, Q=Q, t=t, bp4=bp4: e.dma_start(
                            out=s_ub[bp4][:, :],
                            in_=AP(u_d, t * 128 * DM + Q * 512, [[DM, 128], [1, 512]])),
                            waits=w, inc=(suld[bp4], 16))
                    w2w = (w2ld[0], 32 + 16 * (sidx // 2 + 1)) if par == 0 \
                        else (w2ld[1], 16 * ((sidx + 1) // 2))
                    for ht in range(43):
                        w = []
                        if ht == 0:
                            w = [w2w, (gpl[par2], 16 * (idx3 // 3 + 1))]
                            if idx3 >= 3:
                                w.append((yev, idx3 - 2))
                            if first_mm_p2:
                                w += [(gvec, 4 * NH), (silS, 4 * NH)]
                                first_mm_p2 = False
                        lastop = (ht == 42 and hf == 0 and Q != 0)
                        P.op("tensor", lambda e, par=par, par2=par2, ht=ht, bank=bank, hf=hf: e.matmul(
                            p_y[bank][:, :], s_gp[par2][:, ht * 128:(ht + 1) * 128],
                            s_w2s[par][:, ht * 512:(ht + 1) * 512],
                            start=(ht == 0), stop=(ht == 42 and hf == 0)),
                            waits=w, inc=(ymm if lastop else None))
                    if Q == 0:
                        for ht in range(43):
                            hg = hf * 43 + ht
                            lastop = (ht == 42)
                            inc = None
                            if lastop:
                                inc = ymm if hf == 0 else a2rq
                            a2st = (hg == 0 and t % 4 == 0)
                            a2sp = (hg == NH - 1 and t % 4 == 3)
                            P.op("tensor", lambda e, hg=hg, ht=ht, par2=par2, t=t,
                                 a2st=a2st, a2sp=a2sp: e.matmul(
                                p_a2[:, t * 128:(t + 1) * 128],
                                s_a2T[:, hg * R:(hg + 1) * R],
                                s_gp[par2][:, ht * 128:(ht + 1) * 128],
                                start=a2st, stop=a2sp, skip_group_check=True),
                                inc=inc)
                    if hf == 1:
                        if Q == 0:
                            P.op("scalar", lambda e, t=t: e.activation(
                                s_a2s[:, t * 128:(t + 1) * 128],
                                p_a2[:, t * 128:(t + 1) * 128], AF.Copy, scale=2.0),
                                waits=[(a2rq, t + 1)], inc=a2ev)
                        P.op("tensor", lambda e, t=t, Q=Q, bank=bank: e.matmul(
                            p_y[bank][:, :], s_a2s[:, t * 128:(t + 1) * 128],
                            s_b2T[:, Q * 512:(Q + 1) * 512], start=False, stop=True),
                            waits=([(a2ev, t + 1)] if Q == 0 else []), inc=ymm)
                    if hf == 0:
                        P.op("vector", lambda e, t=t, bank=bank: e.tensor_copy(
                            s_yac[:, t * 512:(t + 1) * 512], p_y[bank][:, :]),
                            waits=[(ymm, idx3 + 1)], inc=yev)
                    else:
                        w = [(ymm, idx3 + 1)]
                        if idx4 >= 2:
                            w.append((sil2, idx4 - 1))
                        P.op("vector", lambda e, t=t, bank=bank, idx4=idx4: e.tensor_tensor(
                            s_e1[idx4 % 2][:, :], s_yac[:, t * 512:(t + 1) * 512],
                            p_y[bank][:, :], op=ALU.add),
                            waits=w, inc=yev)
                        P.op("scalar", lambda e, idx4=idx4: e.activation(
                            s_s2[idx4 % 2][:, :], s_e1[idx4 % 2][:, :], AF.Silu),
                            waits=[(yev, idx3 + 1)] +
                                  ([(p2v, idx4 - 1)] if idx4 >= 2 else []),
                            inc=sil2)
                        P.op("vector", lambda e, idx4=idx4: e.tensor_tensor(
                            s_e1[idx4 % 2][:, :], s_s2[idx4 % 2][:, :],
                            s_sb[idx4 % 2][:, :], op=ALU.mult),
                            waits=[(sil2, idx4 + 1),
                                   (suld[idx4 % 2], 32 * (idx4 // 2 + 1))], inc=p2v)
                        P.op("vector", lambda e, idx4=idx4: e.tensor_tensor(
                            s_ob[idx4 % 2][:, :], s_e1[idx4 % 2][:, :],
                            s_ub[idx4 % 2][:, :], op=ALU.add),
                            waits=[(p2v, idx4 + 1)] +
                                  ([(outst[idx4 % 2], 16 * (idx4 // 2))] if idx4 >= 2 else []),
                            inc=epi)
                        P.op("scalar", lambda e, Q=Q, t=t, idx4=idx4: e.dma_start(
                            out=AP(out_d, t * 128 * DM + Q * 512, [[DM, 128], [1, 512]]),
                            in_=s_ob[idx4 % 2][:, :]),
                            waits=[(epi, idx4 + 1)], inc=(outst[idx4 % 2], 16))
        assert outst[0].n == 16 * 64 and outst[1].n == 16 * 64 and epi.n == 128

        P.op("sync", lambda e: e.nop(),
             waits=[(outst[0], 16 * 64), (outst[1], 16 * 64)])

        P.check(sems)

        for s in sems:
            s.h = E(nc.semaphore(s.name))

        block = E(nc.Block())

        def run_eng(e, name):
            for (en, fn, waits, incs) in P.ops:
                if en != name:
                    continue
                for (s, v) in waits:
                    e.wait_ge(s.h, v)
                ins = fn(e)
                for (s, k) in incs:
                    ins.then_inc(s.h, k)

        block.sync(lambda e: run_eng(e, "sync"))
        block.gpsimd(lambda e: run_eng(e, "gpsimd"))
        block.tensor(lambda e: run_eng(e, "tensor"))
        block.vector(lambda e: run_eng(e, "vector"))
        block.scalar(lambda e: run_eng(e, "scalar"))

    return nc


def prep_inputs(inputs):
    f32c = lambda x: np.ascontiguousarray(np.asarray(x), np.float32)
    u = f32c(inputs["L0_attn_u"]).reshape(16384, DM)
    w1T = f32c(np.asarray(inputs["L0_ffn_w1"]).T)
    w3T = f32c(np.asarray(inputs["L0_ffn_w3"]).T)
    w2T = f32c(np.asarray(inputs["L0_ffn_w2"]).T)
    def packT(a, ntile):
        at = np.asarray(a, np.float32).T.reshape(ntile, 128, R)
        return np.ascontiguousarray(at.transpose(1, 0, 2).reshape(128, ntile * R))
    a1T = packT(inputs["L0_ffn_lora_A1"], ND)
    a3T = packT(inputs["L0_ffn_lora_A3"], ND)
    a2T = packT(inputs["L0_ffn_lora_A2"], NH)
    b1T = f32c(np.asarray(inputs["L0_ffn_lora_B1"]).T)
    b3T = f32c(np.asarray(inputs["L0_ffn_lora_B3"]).T)
    b2T = f32c(np.asarray(inputs["L0_ffn_lora_B2"]).T)
    lam = f32c(inputs["L0_attn_Lambda_ri"])
    pri = f32c(inputs["L0_attn_p_ri"])
    qri = f32c(inputs["L0_attn_q_ri"])
    vc = f32c(inputs["L0_attn_Vc_ri"]).reshape(64, 128)
    ct = f32c(inputs["L0_attn_Ct_ri"]).reshape(64, 2)
    bp = f32c(inputs["L0_attn_Bp"]).reshape(1, 64)
    dd = f32c(inputs["L0_attn_D"]).reshape(1, 1)
    ls = f32c(inputs["L0_attn_log_step"]).reshape(1, 1)

    shared = dict(w1T=w1T, w3T=w3T, w2T=w2T, a1T=a1T, a3T=a3T, a2T=a2T,
                  b1T=b1T, b3T=b3T, b2T=b2T, lam=lam, pri=pri, qri=qri,
                  vc=vc, ct=ct, bp=bp, dd=dd, ls=ls)
    return shared, u


def kernel(**inputs):
    shared, u = prep_inputs(inputs)
    nc = build_nc()
    in_maps = [dict(shared, u=np.ascontiguousarray(u[c * NTOK:(c + 1) * NTOK]))
               for c in range(NCORE)]
    res = run_bass_kernel_spmd(nc, in_maps, core_ids=list(range(NCORE)))
    out = np.concatenate([np.asarray(r["out"]) for r in res.results], axis=0)
    return out.reshape(2, 8192, DM)


if __name__ == "__main__":
    nc = build_nc()
    print("built OK, n_inst:", nc.n_instructions())
